# version 04
# baseline (speedup 1.0000x reference)
import numpy as np
import concourse.bass as bass
import concourse.mybir as mybir
from concourse.bass_utils import run_bass_kernel_spmd

F32 = mybir.dt.float32
BF16 = mybir.dt.bfloat16
U8 = mybir.dt.uint8
AF = mybir.ActivationFunctionType
ALU = mybir.AluOpType
AX = mybir.AxisListType

N_CORES = 8
B, T, D = 4, 4096, 1024
TOK = 2048
NT = B * T



class Sem:
    __slots__ = ("h", "n")

    def __init__(self, h):
        self.h = h
        self.n = 0


class Prog:
    ENGINES = ("tensor", "vector", "scalar", "gpsimd", "sync")

    def __init__(self, nc):
        self.nc = nc
        self.q = {e: [] for e in self.ENGINES}
        self._semid = 0
        self.joinsems = ()

    def sem(self, name=None):
        h = self.nc.alloc_semaphore(name or f"s{self._semid}")
        self._semid += 1
        return Sem(h)

    def emit(self, eng, fn, wait=(), inc=()):
        if SERIAL_ALL and eng in ("tensor", "vector", "scalar", "gpsimd"):
            wait = list(wait) + [(s, s.n) for s in self.joinsems if s.n]
        wait = tuple((s, v) for s, v in wait)
        inc = tuple(inc)
        for s, a in inc:
            s.n += a

        def run(e, fn=fn, wait=wait, inc=inc):
            for s, v in wait:
                e.wait_ge(s.h, v)
            inst = fn(e)
            for s, a in inc:
                inst = inst.then_inc(s.h, a)

        self.q[eng].append(run)

    def wait(self, eng, sem, val):
        self.q[eng].append(lambda e, s=sem, v=val: e.wait_ge(s.h, v))

    def dma(self, eng, out, in_, sem, wait=()):
        return self.emit(eng, lambda e: e.dma_start(out=out, in_=in_),
                         wait=wait, inc=((sem, 16),))

    def finalize(self):
        with self.nc.Block() as block:
            for name in self.ENGINES:
                ops = self.q[name]
                if not ops:
                    continue
                getattr(block, name)(lambda eng, ops=ops: [op(eng) for op in ops])


def new_nc():
    return bass.Bass("TRN2", target_bir_lowering=False,
                     detect_race_conditions=False)



STOP_PHASE = 99
SERIAL_ALL = False
SKIP3 = ''
SERIAL3 = True
DBG_OUT = False
PAD_OPS = 0
PAD_MM = 0


def build_mega():
    nc = new_nc()
    p = Prog(nc)

    def din(name, shape, dt=F32):
        return nc.dram_tensor(name, shape, dt, kind="ExternalInput").ap()

    xb = din("xb", [T, D])
    xown = din("xown", [TOK, D])
    wqk = din("wqk", [8, D, 128])
    wvh = din("wvh", [8, D, 64])
    w1i = din("w1i", [D, 4096])
    b1i = din("b1i", [128, 32])
    w2i = din("w2i", [4096, D])
    b2i = din("b2i", [1, D])
    gami = din("gami", [1, D])
    beti = din("beti", [1, D])
    wproji = din("wproji", [D, D])
    rwi = din("rwi", [D, 8])
    rbi = din("rbi", [1, 8])
    wq4 = din("wq4", [D, 4, 256])
    wk4 = din("wk4", [D, 4, 256])
    wv4 = din("wv4", [D, 4, 256])
    wf4 = din("wf4", [D, 4, 64])
    gnwi = din("gnwi", [1, 256])
    woi = din("woi", [D, D])
    ew1t = din("ew1t", [D, 2816])
    ew3t = din("ew3t", [D, 2816])
    ew2t = din("ew2t", [2816, D])
    identi = din("identi", [128, 128], BF16)
    identfi = din("identfi", [128, 128], F32)
    triui = din("triui", [128, 128], BF16)
    r127i = din("r127i", [128, 128], F32)
    eyerowi = din("eyerowi", [16, 2048], F32)
    flagsi = din("flagsi", [128, 32], F32)
    flagwei = din("flagwei", [128, 64], F32)

    out_ext = nc.dram_tensor("out", [TOK, D], F32, kind="ExternalOutput").ap()

    def dint(name, shape, dt, shared=False):
        if shared:
            return nc.dram_tensor(name, shape, dt, addr_space="Shared").ap()
        return nc.dram_tensor(name, shape, dt).ap()

    if DBG_OUT:
        h1sp = nc.dram_tensor("h1sp", [32, 128, TOK], BF16,
                              kind="ExternalOutput").ap()
        o0own_d = nc.dram_tensor("o0own_d", [4, 128, TOK], BF16,
                                 kind="ExternalOutput").ap()
    else:
        h1sp = dint("h1sp", [32, 128, TOK], BF16)
        o0own_d = dint("o0own_d", [4, 128, TOK], BF16)
    if DBG_OUT:
        x1sp = nc.dram_tensor("x1sp", [TOK, D], F32, kind="ExternalOutput").ap()
        s1sp = nc.dram_tensor("s1sp", [TOK, D], F32, kind="ExternalOutput").ap()
    else:
        x1sp = dint("x1sp", [TOK, D], F32)
        s1sp = dint("s1sp", [TOK, D], F32)
    ag1s = dint("ag1s", [512, TOK], BF16)
    ag1o = dint("ag1o", [4096, TOK], BF16, shared=True)
    ag2s = dint("ag2s", [1024, 16], F32)
    ag2o = dint("ag2o", [8192, 16], F32, shared=True)
    ag3s = [dint(f"ag3s{h}", [128, 384], F32) for h in range(4)]
    ag3o = [dint(f"ag3o{h}", [1024, 384], F32, shared=True) for h in range(4)]
    ag4s = dint("ag4s", [1024, TOK], BF16)
    ag4o = dint("ag4o", [8192, TOK], BF16, shared=True)
    zstage = dint("zstage", [NT, D], BF16)
    rsz = dint("rsz", [TOK, D], BF16)
    rszdbg = (nc.dram_tensor("rszdbg", [TOK, D], BF16,
                             kind="ExternalOutput").ap() if DBG_OUT else None)

    PA = nc.alloc_psum_tensor("PA", [128, 1024], F32)
    PB2 = nc.alloc_psum_tensor("PB2", [128, 1024], F32)
    PCt = nc.alloc_psum_tensor("PCt", [128, 2048], BF16)
    PD = nc.alloc_psum_tensor("PD", [128, 512], F32)
    PE2 = nc.alloc_psum_tensor("PE2", [128, 512], F32)

    sb = nc.alloc_sbuf_tensor
    ident_s = sb("ident_s", [128, 128], BF16)
    identf_s = sb("identf_s", [128, 128], F32)
    triu_s = sb("triu_s", [128, 128], BF16)
    r127_s = sb("r127_s", [128, 128], F32)
    ones1 = sb("ones1", [1, 128], BF16)
    ones1f = sb("ones1f", [1, 128], F32)
    ones1x64 = sb("ones1x64", [1, 64], BF16)
    flags = sb("flags", [128, 32], F32)
    flagwe = sb("flagwe", [128, 64], F32)
    tw_s = sb("tw_s", [128, 16], F32)
    half_col = sb("half_col", [128, 1], F32)
    iota8 = sb("iota8", [128, 8], F32)
    zeros64 = sb("zeros64", [64, 128], F32)
    z128b = sb("z128b", [128, 128], BF16)
    z64b = sb("z64b", [64, 256], BF16)
    b1s = sb("b1s", [128, 32], F32)
    rws = sb("rws", [128, 8, 8], F32)
    rbs = sb("rbs", [1, 8], F32)
    wexpAll = sb("wexpAll", [128, 8, 16], F32)

    gmc_tensors = []
    for v in (-4.0, 0.125, 1e-5, -1.0, 1.0):
        t = sb(f"cst_{str(v).replace('.', '_').replace('-', 'm')}",
               [128, 1], F32)
        nc.const_aps.register(F32, float(v), t.ap())
        gmc_tensors.append((t, v))

    ARENA_BYTES = 200 * 1024
    AR = sb("ARENA", [128, ARENA_BYTES], U8)
    ar_base = nc.lookup_mloc(AR).addr

    def car(name, off, shape, dt):
        per = 1
        for s in shape[1:]:
            per *= s
        per *= mybir.dt.size(dt)
        assert off + per <= ARENA_BYTES, (name, off, per)
        return nc.alloc_sbuf_tensor_at(name, shape, dt, offset=ar_base + off)

    ld = p.sem("ld")
    sto = p.sem("sto")
    sld = p.sem("sld")
    vst = p.sem("vst")
    gld = p.sem("gld")
    cc = p.sem("cc")
    mm = p.sem("mm")
    vch = p.sem("vch")
    sch = p.sem("sch")
    tr = p.sem("tr")
    ev = p.sem("ev")
    gmc = p.sem("gmc")
    ALLSEMS = (ld, sto, sld, vst, gld, mm, vch, sch, tr, ev, gmc)
    p.joinsems = (mm, vch, sch, tr, ev, gmc)

    def barrier():
        for eng in Prog.ENGINES:
            for s in ALLSEMS:
                if s.n:
                    p.wait(eng, s, s.n)

    dumt = car("dumt", 0, [128, 1024], F32)

    def finish_dummy():
        barrier()
        for _i in range(PAD_OPS):
            p.emit("vector", lambda v: v.tensor_scalar(
                dumt[:], dumt[:], 1.0, None, ALU.mult), inc=((vch, 1),))
        for _i in range(PAD_MM):
            p.emit("tensor", lambda t: t.matmul(
                PD[:, 0:128], ident_s[:], ident_s[:],
                start=True, stop=True), inc=((mm, 1),))
        p.wait("gpsimd", vch, vch.n)
        p.wait("gpsimd", mm, mm.n)
        p.emit("gpsimd", lambda g: g.memset(dumt[:], 1.0), inc=((gmc, 1),))
        gv = gmc.n
        for i in range(16):
            p.dma("sync", out_ext[i * 128:(i + 1) * 128, :], dumt[:], sto,
                  wait=[(gmc, gv)] if i == 0 else [])
        p.wait("sync", sto, sto.n)
        p.finalize()
        return nc

    for t, v in gmc_tensors:
        p.emit("gpsimd", lambda g, t=t, v=v: g.memset(t[:], v), inc=((gmc, 1),))
    for t, v in ((ones1, 1.0), (ones1f, 1.0), (ones1x64, 1.0),
                 (half_col, 0.5), (zeros64, 0.0), (z128b, 0.0), (z64b, 0.0)):
        p.emit("gpsimd", lambda g, t=t, v=v: g.memset(t[:], v), inc=((gmc, 1),))
    p.emit("gpsimd", lambda g: g.iota(
        iota8[:], [[1, 8]], base=10000, channel_multiplier=0,
        allow_small_or_imprecise_dtypes=True), inc=((gmc, 1),))
    p.dma("sync", ident_s[:], identi, ld)
    p.dma("sync", identf_s[:], identfi, ld)
    p.dma("sync", triu_s[:], triui, ld)
    p.dma("sync", r127_s[:], r127i, ld)
    p.dma("sync", flags[:], flagsi, ld)
    p.dma("sync", flagwe[:], flagwei, ld)
    p.dma("sync", b1s[:], b1i, ld)
    p.dma("sync", rws[:], rwi.rearrange("(k p) e -> p k e", p=128), ld)
    p.dma("sync", rbs[:], rbi, ld)
    setup_ld = ld.n
    setup_g = gmc.n
    for eng in ("tensor", "vector", "scalar"):
        p.wait(eng, ld, setup_ld)
        p.wait(eng, gmc, setup_g)
    fOL = flags[:, 16:17]
    fOR = flags[:, 17:18]

    def emit_ln_tiles(n_tiles, load_tile, out_tile, lnd):
        Dd = lnd["D"]
        done_tok = []
        for i in range(n_tiles):
            pb = i % 2
            x, xsq, st, xn = (lnd["x"][pb], lnd["xsq"][pb], lnd["st"][pb],
                              lnd["xn"][pb])
            waits = []
            if i >= 2:
                waits.append(done_tok[i - 2])
            load_tile(i, x, waits)
            ldv = ld.n
            p.emit("vector", lambda v, x=x, st=st:
                   v.reduce_sum(st[:, 0:1], x[:], axis=AX.X),
                   wait=[(ld, ldv)], inc=((vch, 1),))
            p.emit("scalar", lambda s, x=x, xsq=xsq, st=st:
                   s.activation(xsq[:], x[:], AF.Square, accum_out=st[:, 1:2]),
                   wait=[(ld, ldv)], inc=((sch, 1),))
            p.emit("vector", lambda v, st=st: v.tensor_scalar(
                st[:, 2:3], st[:, 0:1], 1.0 / Dd, None, ALU.mult),
                wait=[(vch, vch.n)], inc=((vch, 1),))
            p.emit("vector", lambda v, st=st: v.tensor_scalar(
                st[:, 3:4], st[:, 1:2], 1.0 / Dd, None, ALU.mult),
                wait=[(sch, sch.n)], inc=((vch, 1),))
            p.emit("vector", lambda v, st=st: v.tensor_tensor(
                st[:, 4:5], st[:, 2:3], st[:, 2:3], ALU.mult),
                wait=[(vch, vch.n)], inc=((vch, 1),))
            p.emit("vector", lambda v, st=st: v.tensor_tensor(
                st[:, 5:6], st[:, 3:4], st[:, 4:5], ALU.subtract),
                wait=[(vch, vch.n)], inc=((vch, 1),))
            p.emit("scalar", lambda s, st=st: s.activation(
                st[:, 6:7], st[:, 5:6], AF.Sqrt, bias=1e-5),
                wait=[(vch, vch.n)], inc=((sch, 1),))
            p.emit("vector", lambda v, st=st: v.reciprocal(
                st[:, 7:8], st[:, 6:7]),
                wait=[(sch, sch.n)], inc=((vch, 1),))
            p.emit("vector", lambda v, x=x, xn=xn, st=st: v.tensor_scalar(
                xn[:], x[:], st[:, 2:3], st[:, 7:8], ALU.subtract, ALU.mult),
                wait=[(vch, vch.n)], inc=((vch, 1),))
            done_tok.append(out_tile(i, xn, (vch, vch.n)))
        return done_tok

    NTILE = T // 128
    xnT = car("xnT", 0, [128, 8, T], BF16)
    osb = car("osb", 65536, [128, 2, T], BF16)
    obo = [car(f"obo{i}", 81920 + i * 4096, [128, TOK], BF16)
           for i in range(2)]
    obp = [car(f"obp{i}", 90112 + i * 4096, [128, TOK], BF16)
           for i in range(2)]
    obx = [car(f"obx{i}", 172288 + i * 4096, [128, TOK], BF16)
           for i in range(2)]
    qT = car("qT", 98304, [64, T], BF16)
    kT = car("kT", 106496, [64, T], BF16)
    vT = car("vT", 114688, [64, T], BF16)
    vt = car("vt", 122880, [128, NTILE, 65], BF16)
    ptb = [car(f"ptb{i}", 127104 + i * 2048, [128, 1024], BF16)
           for i in range(2)]
    wqk_s = [car(f"wqk_s{i}", 131200 + i * 2048, [128, 8, 128], BF16)
             for i in range(2)]
    wv_s = [car(f"wv_s{i}", 135296 + i * 1024, [128, 8, 64], BF16)
            for i in range(2)]
    ln1 = {
        "D": D,
        "x": [car(f"a_lnx{i}", 137344 + i * 4096, [128, D], F32)
              for i in range(2)],
        "xsq": [car(f"a_lnq{i}", 145536 + i * 4096, [128, D], F32)
                for i in range(2)],
        "st": [car(f"a_lns{i}", 153728 + i * 64, [128, 8], F32)
               for i in range(2)],
        "xn": [car(f"a_lnn{i}", 153856 + i * 2048, [128, D], BF16)
               for i in range(2)],
    }
    otmp = car("otmp", 157952, [64, 1024], F32)
    lrow = car("lrow", 162048, [1, 1024], F32)
    lrow2 = car("lrow2", 166144, [1, 1024], F32)
    lrowb = car("lrowb", 170240, [1, 1024], BF16)

    p.emit("gpsimd", lambda g: g.memset(vt[:], 1.0), inc=((gmc, 1),))
    p.wait("tensor", gmc, gmc.n)
    p.wait("scalar", gmc, gmc.n)

    def a_load(i, xbuf, waits):
        p.dma("sync", xbuf[:], xb[i * 128:(i + 1) * 128, :], ld, wait=waits)

    def a_out(i, xn_ap, ready):
        for k in range(8):
            half = k % 2
            w = [ready, (ld, setup_ld)]
            if i > 0 or k >= 2:
                w.append((tr, tr.n - 1))
            p.emit("tensor", lambda t, xn_ap=xn_ap, k=k, half=half:
                   t.transpose(PCt[:, half * 1024:half * 1024 + 128],
                               xn_ap[:, k * 128:(k + 1) * 128], ident_s[:]),
                   wait=w, inc=((mm, 1),))
            mv = mm.n
            p.emit("scalar", lambda s, i=i, k=k, half=half:
                   s.copy(xnT[:, k, i * 128:(i + 1) * 128],
                          PCt[:, half * 1024:half * 1024 + 128]),
                   wait=[(mm, mv)], inc=((tr, 1),))
        return (tr, tr.n)

    emit_ln_tiles(NTILE, a_load, a_out, ln1)
    xnT_ready = (tr, tr.n)

    p.dma("gpsimd", wqk_s[0][:], wqk[0].rearrange("(k p) c -> p k c", p=128),
          gld)
    p.dma("gpsimd", wv_s[0][:], wvh[0].rearrange("(k p) c -> p k c", p=128),
          gld)

    ext_tok = {}
    NQG = 4
    for h in range(8):
        hp = h % 2
        wq_ = wqk_s[hp]
        wvs = wv_s[hp]
        wldv = gld.n

        for ts in range(8):
            waits = [(gld, wldv), xnT_ready]
            if ts > 0:
                waits.append((tr, tr.n))
            for k in range(8):
                p.emit("tensor", lambda t, k=k, ts=ts, wq_=wq_:
                       t.matmul(PA[:, 0:512], wq_[:, k, :],
                                xnT[:, k, ts * 512:(ts + 1) * 512],
                                start=(k == 0), stop=(k == 7)),
                       wait=waits if k == 0 else [],
                       inc=((mm, 1),) if k == 7 else ())
            mv = mm.n
            p.emit("scalar", lambda s, ts=ts:
                   s.copy(qT[:, ts * 512:(ts + 1) * 512], PA[0:64, 0:512]),
                   wait=[(mm, mv)], inc=((sch, 1),))
            p.emit("vector", lambda v, ts=ts:
                   v.tensor_copy(kT[:, ts * 512:(ts + 1) * 512],
                                 PA[64:128, 0:512]),
                   wait=[(mm, mv)], inc=((tr, 1),))
            p.wait("tensor", sch, sch.n)

        for ts in range(8):
            waits = [(tr, tr.n)]
            for k in range(8):
                p.emit("tensor", lambda t, k=k, ts=ts, wvs=wvs:
                       t.matmul(PA[0:64, 0:512], wvs[:, k, :],
                                xnT[:, k, ts * 512:(ts + 1) * 512],
                                start=(k == 0), stop=(k == 7)),
                       wait=waits if k == 0 else [],
                       inc=((mm, 1),) if k == 7 else ())
            mv = mm.n
            p.emit("vector", lambda v, ts=ts:
                   v.tensor_copy(vT[:, ts * 512:(ts + 1) * 512],
                                 PA[0:64, 0:512]),
                   wait=[(mm, mv)], inc=((tr, 1),))
        v_done_tr = tr.n

        if h < 7:
            p.dma("gpsimd", wqk_s[(h + 1) % 2][:],
                  wqk[h + 1].rearrange("(k p) c -> p k c", p=128), gld,
                  wait=[(mm, mm.n)])
            p.dma("gpsimd", wv_s[(h + 1) % 2][:],
                  wvh[h + 1].rearrange("(k p) c -> p k c", p=128), gld,
                  wait=[(mm, mm.n)])

        for j in range(NTILE):
            half = j % 2
            w = [(tr, v_done_tr)] if j < 2 else [(tr, tr.n - 1)]
            p.emit("tensor", lambda t, j=j, half=half:
                   t.transpose(PCt[:, half * 1024:half * 1024 + 64],
                               vT[:, j * 128:(j + 1) * 128],
                               ident_s[0:64, 0:64]),
                   wait=w, inc=((mm, 1),))
            p.emit("scalar", lambda s, j=j, half=half:
                   s.copy(vt[:, j, 0:64], PCt[:, half * 1024:half * 1024 + 64]),
                   wait=[(mm, mm.n)], inc=((tr, 1),))
        vt_done = tr.n

        pO = PB2[0:65, :]
        ochunk = (h // 2) % 2
        for g in range(NQG):
            ilo, ihi = 8 * g, 8 * g + 8
            for j in range(ihi):
                i_start = max(j, ilo)
                nq = (ihi - i_start) * 128
                tq0 = i_start * 128
                ptile = ptb[j % 2]
                nchunks = (nq + 511) // 512
                for ci in range(nchunks):
                    cw = min(512, nq - ci * 512)
                    waits = []
                    if ci == 0:
                        if g == 0 and j == 0:
                            waits.append((tr, vt_done))
                        waits.append((sch, sch.n))
                    p.emit("tensor", lambda t, ci=ci, cw=cw, tq0=tq0, j=j:
                           t.matmul(PA[:, ci * 512:ci * 512 + cw],
                                    kT[:, j * 128:(j + 1) * 128],
                                    qT[:, tq0 + ci * 512:tq0 + ci * 512 + cw],
                                    start=True, stop=True),
                           wait=waits, inc=((mm, 1),))
                mmv = mm.n
                p.emit("scalar", lambda s, ptile=ptile, nq=nq:
                       s.activation(ptile[:, 0:nq], PA[:, 0:nq], AF.Exp,
                                    bias=-4.0, scale=0.125),
                       wait=[(mm, mmv)], inc=((sch, 1),))
                expv = sch.n
                pv_wait = [(sch, expv)]
                if j >= ilo:
                    p.emit("vector", lambda v, ptile=ptile:
                           v.tensor_tensor(ptile[:, 0:128], ptile[:, 0:128],
                                           triu_s[:], ALU.mult),
                           wait=[(sch, expv), (ld, setup_ld)], inc=((vch, 1),))
                    pv_wait = [(vch, vch.n)]
                if j == 0:
                    for ci in range(2):
                        p.emit("tensor", lambda t, ci=ci, ptile=ptile:
                               t.matmul(pO[:, ci * 512:(ci + 1) * 512],
                                        vt[:, 0, :],
                                        ptile[:, ci * 512:(ci + 1) * 512],
                                        start=True, stop=False,
                                        skip_group_check=True),
                               wait=pv_wait if ci == 0 else [],
                               inc=((mm, 1),) if ci == 1 else ())
                else:
                    n_i = nq // 128
                    for ii in range(n_i):
                        iblk = i_start + ii
                        col0 = (iblk - ilo) * 128
                        p.emit("tensor", lambda t, ii=ii, col0=col0, j=j,
                               iblk=iblk, ptile=ptile:
                               t.matmul(pO[:, col0:col0 + 128], vt[:, j, :],
                                        ptile[:, ii * 128:(ii + 1) * 128],
                                        start=False, stop=(j == iblk),
                                        skip_group_check=True),
                               wait=pv_wait if ii == 0 else [],
                               inc=((mm, 1),) if ii == n_i - 1 else ())
            grp_done = mm.n
            p.emit("vector", lambda v: v.tensor_copy(lrow[:], pO[64:65, :]),
                   wait=[(mm, grp_done)], inc=((vch, 1),))
            p.emit("vector", lambda v: v.reciprocal(lrow2[:], lrow[:]),
                   wait=[(vch, vch.n)], inc=((vch, 1),))
            p.emit("vector", lambda v: v.tensor_copy(lrowb[:], lrow2[:]),
                   wait=[(vch, vch.n)], inc=((vch, 1),))
            linv_v = vch.n
            p.emit("tensor", lambda t: t.matmul(
                PD[0:64, :], ones1x64[:], lrowb[:, 0:512],
                start=True, stop=True),
                wait=[(vch, linv_v), (gmc, setup_g)])
            p.emit("tensor", lambda t: t.matmul(
                PE2[0:64, :], ones1x64[:], lrowb[:, 512:1024],
                start=True, stop=True), inc=((mm, 1),))
            bc_v = mm.n
            p.emit("scalar", lambda s: s.copy(otmp[:], pO[0:64, :]),
                   wait=[(mm, bc_v)], inc=((sch, 1),))
            ow = [(sch, sch.n)]
            if h >= 4 and g == 0:
                ow.append((vch, ext_tok[h // 2 - 2]))
            p.emit("vector", lambda v, h=h, g=g, ochunk=ochunk:
                   v.tensor_tensor(
                       osb[(h % 2) * 64:(h % 2) * 64 + 64, ochunk,
                           g * 1024:(g + 1) * 1024][:, 0:512],
                       otmp[:, 0:512], PD[0:64, :], ALU.mult),
                   wait=ow, inc=((vch, 1),))
            p.emit("vector", lambda v, h=h, g=g, ochunk=ochunk:
                   v.tensor_tensor(
                       osb[(h % 2) * 64:(h % 2) * 64 + 64, ochunk,
                           g * 1024:(g + 1) * 1024][:, 512:1024],
                       otmp[:, 512:1024], PE2[0:64, :], ALU.mult),
                   inc=((vch, 1),))
            p.wait("tensor", sch, sch.n)
            p.wait("tensor", vch, vch.n)

        if h % 2 == 1:
            jch = h // 2
            pb = jch % 2
            hd = vch.n
            p.emit("vector", lambda v, pb=pb: v.tensor_scalar(
                obo[pb][:], osb[:, pb, 0:TOK], fOL, None, ALU.mult),
                wait=[(vch, hd)], inc=((vch, 1),))
            p.emit("vector", lambda v, pb=pb: v.tensor_scalar(
                obx[pb][:], osb[:, pb, TOK:T], fOR, None, ALU.mult),
                wait=[(vch, vch.n)], inc=((vch, 1),))
            p.emit("vector", lambda v, pb=pb: v.tensor_tensor(
                obo[pb][:], obo[pb][:], obx[pb][:], ALU.add),
                wait=[(vch, vch.n)], inc=((vch, 1),))
            obo_v = vch.n
            p.emit("scalar", lambda s, pb=pb: s.mul(
                obp[pb][:], osb[:, pb, 0:TOK], fOR),
                wait=[(vch, hd)], inc=((sch, 1),))
            p.emit("vector", lambda v, pb=pb: v.tensor_scalar(
                obx[pb][:], osb[:, pb, TOK:T], fOL, None, ALU.mult),
                wait=[(vch, obo_v)], inc=((vch, 1),))
            p.emit("vector", lambda v, pb=pb: v.tensor_tensor(
                obp[pb][:], obp[pb][:], obx[pb][:], ALU.add),
                wait=[(vch, vch.n), (sch, sch.n)], inc=((vch, 1),))
            ext_tok[jch] = vch.n
            p.dma("gpsimd", o0own_d[jch], obo[pb][:], vst,
                  wait=[(vch, obo_v)])
            p.dma("gpsimd",
                  ag1s.rearrange("(c p) t -> p c t", p=128)[:, jch, :],
                  obp[pb][:], vst, wait=[(vch, vch.n)])
    p.emit("gpsimd", lambda g: g.collective_compute(
        "AllGather", ALU.bypass, replica_groups=[list(range(N_CORES))],
        ins=[ag1s], outs=[ag1o]), wait=[(vst, vst.n)], inc=((cc, 1),))
    CC_AG1 = cc.n
    if STOP_PHASE <= 1:
        return finish_dummy()

    xno = car("xno", 65536, [128, 8, TOK], BF16)
    xnt_tmp = car("xnt_tmp", 98304, [128, 8, TOK], BF16)
    w1s_u = [car(f"w1s_u{i}", 131072 + i * 2048, [128, 8, 128], BF16)
             for i in range(2)]
    h1ev = [car(f"h1ev{i}", 135296 + i * 4096, [128, TOK], BF16)
            for i in range(2)]

    pv = vch.n
    p.emit("vector", lambda v: v.tensor_scalar(
        xnt_tmp[:], xnT[:, :, 0:TOK], fOL, None, ALU.mult),
        wait=[(vch, pv), (tr, tr.n), (mm, mm.n)], inc=((vch, 1),))
    p.emit("vector", lambda v: v.scalar_tensor_tensor(
        xno[:], xnT[:, :, TOK:T], fOR, xnt_tmp[:], ALU.mult, ALU.add),
        wait=[(vch, vch.n), (vst, vst.n)], inc=((vch, 1),))
    xno_v = vch.n
    vst_base_up = vst.n

    p.dma("gpsimd", w1s_u[0][:],
          w1i[:, 0:128].rearrange("(k p) f -> p k f", p=128), gld,
          wait=[(mm, mm.n)])
    for f in range(32):
        pb = f % 2
        wldv = gld.n
        if f < 31:
            p.dma("gpsimd", w1s_u[(f + 1) % 2][:],
                  w1i[:, (f + 1) * 128:(f + 2) * 128]
                  .rearrange("(k p) f -> p k f", p=128), gld,
                  wait=[(mm, mm.n)] if f >= 1 else [])
        for gsel, ps_ in ((0, PA), (1, PB2)):
            for k in range(8):
                waits = []
                if k == 0:
                    waits = [(gld, wldv), (vch, xno_v)]
                    if f >= 1:
                        waits.append((vch, vch.n))
                for ci in range(2):
                    p.emit("tensor", lambda t, k=k, ci=ci, gsel=gsel, ps_=ps_,
                           pb=pb:
                           t.matmul(ps_[:, ci * 512:(ci + 1) * 512],
                                    w1s_u[pb][:, k, :],
                                    xno[:, k, gsel * 1024 + ci * 512:
                                        gsel * 1024 + (ci + 1) * 512],
                                    start=(k == 0), stop=(k == 7)),
                           wait=waits if (k == 0 and ci == 0) else [],
                           inc=((mm, 1),) if (k == 7 and ci == 1) else ())
        mv = mm.n
        hw = [(mm, mv)]
        if f >= 2:
            hw.append((vst, vst_base_up + 16 * (f - 1)))
        p.emit("vector", lambda v, f=f, pb=pb: v.tensor_scalar(
            h1ev[pb][:, 0:1024], PA[:], b1s[:, f:f + 1], 0.0,
            ALU.add, ALU.max), wait=hw, inc=((vch, 1),))
        p.emit("vector", lambda v, f=f, pb=pb: v.tensor_scalar(
            h1ev[pb][:, 1024:2048], PB2[:], b1s[:, f:f + 1], 0.0,
            ALU.add, ALU.max), wait=[(mm, mv)], inc=((vch, 1),))
        p.dma("gpsimd", h1sp[f], h1ev[pb][:], vst, wait=[(vch, vch.n)])
    h1_spilled = vst.n
    if STOP_PHASE <= 2:
        return finish_dummy()

    barrier()

    xn1T = car("xn1T", 0, [128, 8, TOK], BF16)
    w2s = car("w2s", 32768, [128, 32, 1024], BF16)
    wps = car("wps", 98304, [128, 8, 1024], BF16)
    h1tl = [car(f"h1tl{i}", 114688 + i * 8192, [128, 32, 128], BF16)
            for i in range(2)]
    gam = car("gam", 131072, [1, D], F32)
    bet = car("bet", 135168, [1, D], F32)
    gamb = car("gamb", 139264, [128, D], F32)
    betb = car("betb", 143360, [128, D], F32)
    b2s = car("b2s", 147456, [1, D], BF16)
    ln2 = {
        "D": D,
        "x": [car(f"d_lnx{i}", 149504 + i * 4096, [128, D], F32)
              for i in range(2)],
        "st": [car(f"d_lns{i}", 157696 + i * 64, [128, 8], F32)
               for i in range(2)],
    }
    sqscr = car("sqscr", 157952, [128, D], F32)
    ln2["xsq"] = [sqscr, sqscr]
    xnf = [car(f"xnf{i}", 162048 + i * 4096, [128, D], F32) for i in range(2)]
    ln2["xn"] = xnf
    st23 = [car(f"st23_{i}", 170240 + i * 64, [128, 16], F32)
            for i in range(2)]
    llb = car("llb", 170496, [128, D], F32)
    ggb = car("ggb", 174592, [128, D], F32)
    x1s = [car(f"x1s{i}", 178688 + i * 4096, [128, D], F32) for i in range(2)]
    xnTf = [car(f"xnTf{i}", 186880 + i * 4096, [128, 8, 128], F32)
            for i in range(2)]
    rrb = [car(f"rrb{i}", 195072 + i * 128, [128, 32], F32) for i in range(2)]
    o0ot = [car(f"o0ot{i}", 195328 + i * 1024, [128, 4, 128], BF16)
            for i in range(2)]
    o0sl = car("o0sl", 197376, [128, 4, 128], BF16)
    o0ac = [car(f"o0ac{i}", 198400 + i * 1024, [128, 4, 128], BF16)
            for i in range(2)]

    p.dma("gpsimd", wps[:], wproji.rearrange("(k p) c -> p k c", p=128), gld)
    p.dma("gpsimd", gam[:], gami, gld)
    p.dma("gpsimd", bet[:], beti, gld)
    p.dma("gpsimd", b2s[:], b2i, gld)
    w2_base = gld.n
    for cgrp in range(8):
        p.dma("gpsimd", w2s[:, cgrp * 4:(cgrp + 1) * 4, :],
              w2i[cgrp * 512:(cgrp + 1) * 512, :]
              .rearrange("(f p) c -> p f c", p=128), gld)
    gb_ld = gld.n
    p.wait("tensor", gld, w2_base)
    for half in range(2):
        p.emit("tensor", lambda t, half=half: t.matmul(
            PA[:, half * 512:(half + 1) * 512], ones1f[:],
            gam[:, half * 512:(half + 1) * 512], start=True, stop=True),
            inc=((mm, 1),))
    p.emit("vector", lambda v: v.tensor_copy(gamb[:], PA[:]),
           wait=[(mm, mm.n)], inc=((vch, 1),))
    p.wait("tensor", vch, vch.n)
    for half in range(2):
        p.emit("tensor", lambda t, half=half: t.matmul(
            PA[:, half * 512:(half + 1) * 512], ones1f[:],
            bet[:, half * 512:(half + 1) * 512], start=True, stop=True),
            inc=((mm, 1),))
    p.emit("vector", lambda v: v.tensor_copy(betb[:], PA[:]),
           wait=[(mm, mm.n)], inc=((vch, 1),))
    gb_ready = (vch, vch.n)

    def d_load(i, xbuf, waits):
        p.dma("sync", xbuf[:], xown[i * 128:(i + 1) * 128, :], ld, wait=waits)
        hb = h1tl[i % 2]
        hw = [(vst, h1_spilled)]
        if i >= 2:
            hw.append((mm, mm.n))
        for f in range(32):
            p.dma("sync", hb[:, f, :], h1sp[f][:, i * 128:(i + 1) * 128],
                  ld, wait=hw if f == 0 else [])
        ow = [(mm, mm.n)] if i >= 2 else []
        p.dma("sync", o0ot[i % 2][:],
              o0own_d[:, :, i * 128:(i + 1) * 128]
              .rearrange("c p t -> p c t"), sld, wait=ow)

    smv_hist = []
    x1spv = []

    def d_out(i, xn_ap, ready):
        pb = i % 2
        ldv = ld.n
        sldv = sld.n
        st2 = st23[pb]
        r = rrb[pb]
        p.wait("sync", vch, vch.n)
        for ri in range(8):
            aw = [(cc, CC_AG1)] if ri == 0 else []
            if ri == 0 and i >= 1:
                aw.append((mm, smv_hist[i - 1]))
            p.dma("sync", o0sl[:],
                  ag1o.rearrange("(r c p) t -> r p c t", r=8, p=128)
                  [ri][:, :, i * 128:(i + 1) * 128], sld, wait=aw)
            if ri == 0:
                p.emit("vector", lambda v: v.tensor_scalar(
                    o0ac[0][:], o0sl[:], flags[:, 0:1], None, ALU.mult),
                    wait=[(sld, sld.n)] + ([(mm, smv_hist[i - 1])]
                                           if i >= 1 else []),
                    inc=((vch, 1),))
            else:
                dst, src = o0ac[ri % 2], o0ac[(ri - 1) % 2]
                p.emit("vector", lambda v, ri=ri, dst=dst, src=src:
                       v.scalar_tensor_tensor(
                           dst[:], o0sl[:], flags[:, ri:ri + 1],
                           src[:], ALU.mult, ALU.add),
                       wait=[(sld, sld.n)], inc=((vch, 1),))
            if ri < 7:
                p.wait("sync", vch, vch.n)
        o0_v = vch.n
        o0fin = o0ac[1]
        for kc in range(8):
            waits = []
            if kc == 0:
                waits = [(sld, sldv), (gld, w2_base), (vch, o0_v)]
                if i >= 1:
                    waits.append((ev, ev.n))
            src = (o0ot[pb][:, kc, :] if kc < 4 else o0fin[:, kc - 4, :])
            for half in range(2):
                p.emit("tensor", lambda t, kc=kc, half=half, src=src:
                       t.matmul(PB2[:, half * 512:(half + 1) * 512], src,
                                wps[:, kc, half * 512:(half + 1) * 512],
                                start=(kc == 0), stop=(kc == 7)),
                       wait=waits if (kc == 0 and half == 0) else [],
                       inc=((mm, 1),) if (kc == 7 and half == 1) else ())
        smv = mm.n
        smv_hist.append(smv)
        for f in range(32):
            waits = []
            if f == 0:
                waits = [(ld, ldv), (gld, w2_base + 16)]
                if i >= 1:
                    waits.append((vch, vch.n))
            elif f % 4 == 0:
                waits = [(gld, w2_base + 16 * (f // 4 + 1))]
            for half in range(2):
                p.emit("tensor", lambda t, f=f, half=half, pb=pb:
                       t.matmul(PA[:, half * 512:(half + 1) * 512],
                                h1tl[pb][:, f, :],
                                w2s[:, f, half * 512:(half + 1) * 512],
                                start=(f == 0), stop=False),
                       wait=waits if half == 0 else [])
        for half in range(2):
            p.emit("tensor", lambda t, half=half: t.matmul(
                PA[:, half * 512:(half + 1) * 512], ones1[:],
                b2s[:, half * 512:(half + 1) * 512], start=False, stop=True),
                inc=((mm, 1),) if half == 1 else ())
        eov = mm.n
        p.emit("vector", lambda v, xn_ap=xn_ap: v.tensor_tensor(
            xn_ap[:], xn_ap[:], PA[:], ALU.add),
            wait=[ready, (mm, eov)], inc=((vch, 1),))
        p.emit("vector", lambda v, xn_ap=xn_ap, st2=st2: v.reduce_sum(
            st2[:, 0:1], xn_ap[:], axis=AX.X),
            wait=[(vch, vch.n)], inc=((vch, 1),))
        p.emit("scalar", lambda s, xn_ap=xn_ap, st2=st2: s.activation(
            sqscr[:], xn_ap[:], AF.Square, accum_out=st2[:, 1:2]),
            wait=[(vch, vch.n)], inc=((sch, 1),))
        p.emit("vector", lambda v, st2=st2: v.tensor_scalar(
            st2[:, 2:3], st2[:, 0:1], 1.0 / D, None, ALU.mult),
            wait=[(vch, vch.n)], inc=((vch, 1),))
        p.emit("vector", lambda v, st2=st2: v.tensor_scalar(
            st2[:, 3:4], st2[:, 1:2], 1.0 / D, None, ALU.mult),
            wait=[(sch, sch.n)], inc=((vch, 1),))
        p.emit("vector", lambda v, st2=st2: v.tensor_tensor(
            st2[:, 4:5], st2[:, 2:3], st2[:, 2:3], ALU.mult),
            wait=[(vch, vch.n)], inc=((vch, 1),))
        p.emit("vector", lambda v, st2=st2: v.tensor_tensor(
            st2[:, 5:6], st2[:, 3:4], st2[:, 4:5], ALU.subtract),
            wait=[(vch, vch.n)], inc=((vch, 1),))
        p.emit("scalar", lambda s, st2=st2: s.activation(
            st2[:, 6:7], st2[:, 5:6], AF.Sqrt, bias=1e-5),
            wait=[(vch, vch.n)], inc=((sch, 1),))
        p.emit("vector", lambda v, st2=st2: v.reciprocal(
            st2[:, 7:8], st2[:, 6:7]),
            wait=[(sch, sch.n)], inc=((vch, 1),))
        p.emit("vector", lambda v, xn_ap=xn_ap, st2=st2: v.tensor_scalar(
            llb[:], xn_ap[:], st2[:, 2:3], st2[:, 7:8],
            ALU.subtract, ALU.mult),
            wait=[(vch, vch.n)], inc=((vch, 1),))
        p.emit("vector", lambda v: v.tensor_tensor(
            llb[:], llb[:], gamb[:], ALU.mult),
            wait=[(vch, vch.n), gb_ready], inc=((vch, 1),))
        p.emit("vector", lambda v: v.tensor_tensor(
            llb[:], llb[:], betb[:], ALU.add),
            wait=[(vch, vch.n)], inc=((vch, 1),))
        p.emit("scalar", lambda s: s.activation(ggb[:], llb[:], AF.Silu),
               wait=[(vch, vch.n)], inc=((sch, 1),))
        gv = sch.n
        xw = [(sch, gv), (mm, smv)]
        if i >= 2:
            xw.append((vst, x1spv[i - 2]))
        p.emit("vector", lambda v, pb=pb: v.tensor_tensor(
            x1s[pb][:], PB2[:], ggb[:], ALU.mult),
            wait=xw, inc=((vch, 1),))
        p.emit("vector", lambda v, pb=pb, xbuf=ln2["x"][pb]: v.tensor_tensor(
            x1s[pb][:], x1s[pb][:], xbuf[:], ALU.add),
            wait=[(vch, vch.n)], inc=((ev, 1),))
        x1v = ev.n
        p.dma("gpsimd", x1sp[i * 128:(i + 1) * 128, :], x1s[pb][:], vst,
              wait=[(ev, x1v)])
        x1spv.append(vst.n)
        st3 = st2
        p.emit("vector", lambda v, pb=pb, st3=st3: v.reduce_sum(
            st3[:, 8:9], x1s[pb][:], axis=AX.X),
            wait=[(ev, x1v)], inc=((vch, 1),))
        p.emit("scalar", lambda s, pb=pb, st3=st3: s.activation(
            sqscr[:], x1s[pb][:], AF.Square, accum_out=st3[:, 9:10]),
            wait=[(ev, x1v), (sch, sch.n)], inc=((sch, 1),))
        p.emit("vector", lambda v, st3=st3: v.tensor_scalar(
            st3[:, 10:11], st3[:, 8:9], 1.0 / D, None, ALU.mult),
            wait=[(vch, vch.n)], inc=((vch, 1),))
        p.emit("vector", lambda v, st3=st3: v.tensor_scalar(
            st3[:, 11:12], st3[:, 9:10], 1.0 / D, None, ALU.mult),
            wait=[(sch, sch.n)], inc=((vch, 1),))
        p.emit("vector", lambda v, st3=st3: v.tensor_tensor(
            st3[:, 12:13], st3[:, 10:11], st3[:, 10:11], ALU.mult),
            wait=[(vch, vch.n)], inc=((vch, 1),))
        p.emit("vector", lambda v, st3=st3: v.tensor_tensor(
            st3[:, 13:14], st3[:, 11:12], st3[:, 12:13], ALU.subtract),
            wait=[(vch, vch.n)], inc=((vch, 1),))
        p.emit("scalar", lambda s, st3=st3: s.activation(
            st3[:, 14:15], st3[:, 13:14], AF.Sqrt, bias=1e-5),
            wait=[(vch, vch.n)], inc=((sch, 1),))
        p.emit("vector", lambda v, st3=st3: v.reciprocal(
            st3[:, 15:16], st3[:, 14:15]),
            wait=[(sch, sch.n)], inc=((vch, 1),))
        p.emit("vector", lambda v, pb=pb, st3=st3: v.tensor_scalar(
            llb[:], x1s[pb][:], st3[:, 10:11], st3[:, 15:16],
            ALU.subtract, ALU.mult),
            wait=[(vch, vch.n)], inc=((vch, 1),))
        xn1_v = vch.n
        for k in range(8):
            pd_ = PD if k % 2 == 0 else PE2
            w = [(vch, xn1_v)]
            if i > 0 or k >= 2:
                w.append((tr, tr.n - 3))
            p.emit("tensor", lambda t, k=k, pd_=pd_:
                   t.transpose(pd_[:, 0:128],
                               llb[:, k * 128:(k + 1) * 128], identf_s[:]),
                   wait=w, inc=((mm, 1),))
            mv = mm.n
            p.emit("scalar", lambda s, k=k, pb=pb, pd_=pd_:
                   s.copy(xnTf[pb][:, k, :], pd_[:, 0:128]),
                   wait=[(mm, mv)], inc=((tr, 1),))
            p.emit("vector", lambda v, k=k, pb=pb, i=i:
                   v.tensor_copy(xn1T[:, k, i * 128:(i + 1) * 128],
                                 xnTf[pb][:, k, :]),
                   wait=[(tr, tr.n)], inc=((tr, 1),))
        xnt_done = tr.n
        p.dma("gpsimd",
              ag4s.rearrange("(k p) t -> p k t", p=128)
              [:, :, i * 128:(i + 1) * 128],
              xn1T[:, :, i * 128:(i + 1) * 128], vst,
              wait=[(tr, xnt_done)])
        for k in range(8):
            p.emit("tensor", lambda t, k=k, pb=pb: t.matmul(
                PB2[:, 512:520], xnTf[pb][:, k, :], rws[:, k, :],
                start=(k == 0), stop=False),
                wait=[(tr, xnt_done), (vch, vch.n)] if k == 0 else [])
        p.emit("tensor", lambda t: t.matmul(
            PB2[:, 512:520], ones1f[:], rbs[:], start=False, stop=True),
            inc=((mm, 1),))
        rmv = mm.n
        p.emit("vector", lambda v, r=r: v.tensor_scalar(
            r[:, 0:8], PB2[:, 512:520], -50.0, 50.0, ALU.max, ALU.min),
            wait=[(mm, rmv)], inc=((vch, 1),))
        p.emit("scalar", lambda s, r=r: s.activation(
            r[:, 8:16], r[:, 0:8], AF.Sigmoid),
            wait=[(vch, vch.n)], inc=((sch, 1),))
        p.emit("vector", lambda v, r=r: v.reduce_max(
            r[:, 16:17], r[:, 8:16], axis=AX.X),
            wait=[(sch, sch.n)], inc=((vch, 1),))
        p.emit("vector", lambda v, r=r: v.tensor_tensor(
            r[:, 24:32], r[:, 8:16],
            r[:, 16:17].broadcast_to([128, 8]), ALU.subtract),
            wait=[(vch, vch.n)], inc=((vch, 1),))
        p.emit("vector", lambda v, r=r: v.tensor_scalar(
            r[:, 0:8], r[:, 24:32], 0.0, None, ALU.is_ge),
            wait=[(vch, vch.n)], inc=((vch, 1),))
        p.emit("vector", lambda v, r=r: v.scalar_tensor_tensor(
            r[:, 24:32], r[:, 0:8], -10000.0, iota8[:], ALU.mult, ALU.add),
            wait=[(vch, vch.n)], inc=((vch, 1),))
        p.emit("vector", lambda v, r=r: v.tensor_reduce(
            r[:, 17:18], r[:, 24:32], axis=AX.X, op=ALU.min),
            wait=[(vch, vch.n)], inc=((vch, 1),))
        p.emit("vector", lambda v, r=r: v.tensor_scalar(
            r[:, 18:19], r[:, 16:17], 1.0 - 1e-8, 1e-8, ALU.min, ALU.max),
            wait=[(vch, vch.n)], inc=((vch, 1),))
        p.emit("vector", lambda v, r=r: v.tensor_scalar(
            r[:, 19:20], r[:, 18:19], 0.5 + 1e-8, None, ALU.add),
            wait=[(vch, vch.n)], inc=((vch, 1),))
        p.emit("vector", lambda v, r=r, i=i: v.tensor_scalar(
            tw_s[:, i:i + 1], r[:, 19:20], 0.5, 2.0, ALU.max, ALU.min),
            wait=[(vch, vch.n)], inc=((vch, 1),))
        p.emit("vector", lambda v, r=r: v.tensor_tensor(
            r[:, 24:32], iota8[:], r[:, 17:18].broadcast_to([128, 8]),
            ALU.subtract),
            wait=[(vch, vch.n)], inc=((vch, 1),))
        p.emit("vector", lambda v, r=r: v.tensor_scalar(
            r[:, 0:8], r[:, 24:32], 10000.0, None, ALU.is_equal),
            wait=[(vch, vch.n)], inc=((vch, 1),))
        p.emit("vector", lambda v, r=r: v.tensor_tensor(
            r[:, 24:32], r[:, 0:8], r[:, 18:19].broadcast_to([128, 8]),
            ALU.mult),
            wait=[(vch, vch.n)], inc=((vch, 1),))
        p.emit("vector", lambda v, r=r: v.tensor_copy(
            r[:, 31:32], half_col[:]),
            wait=[(vch, vch.n)], inc=((vch, 1),))
        p.emit("vector", lambda v, r=r, i=i: v.tensor_copy(
            wexpAll[:, :, i], r[:, 24:32]),
            wait=[(vch, vch.n)], inc=((vch, 1),))
        return (vch, vch.n)

    emit_ln_tiles(16, d_load, d_out, ln2)
    p.dma("gpsimd", ag2s.rearrange("(e p) t -> p e t", p=128), wexpAll[:],
          vst, wait=[(vch, vch.n)])
    if 'ag24' not in SKIP3:
        p.emit("gpsimd", lambda g: g.collective_compute(
            "AllGather", ALU.bypass, replica_groups=[list(range(N_CORES))],
            ins=[ag2s], outs=[ag2o]), wait=[(vst, vst.n)], inc=((cc, 1),))
    CC_AG2 = cc.n
    if 'ag24' not in SKIP3:
        p.emit("gpsimd", lambda g: g.collective_compute(
            "AllGather", ALU.bypass, replica_groups=[list(range(N_CORES))],
            ins=[ag4s], outs=[ag4o]), inc=((cc, 1),))
    CC_AG4 = cc.n
    if STOP_PHASE <= 3:
        return finish_dummy()

    barrier()

    NCH = 16
    o3 = 65536
    wq_s3 = car("wq_s3", o3, [128, 8, 256], BF16)
    wk_s3 = car("wk_s3", o3 + 4096, [128, 8, 256], BF16)
    wv_s3 = car("wv_s3", o3 + 8192, [128, 8, 256], BF16)
    wf_s3 = car("wf_s3", o3 + 12288, [128, 8, 64], BF16)
    QT3 = car("QT3", o3 + 13312, [128, 2, TOK], BF16)
    KTsl = [car(f"KTsl{i}", o3 + 21504 + i * 2048, [128, 2, 512], BF16)
            for i in range(2)]
    VTs3 = [car(f"VTs3{i}", o3 + 25600 + i * 2048, [128, 2, 512], BF16)
            for i in range(2)]
    Ktok = car("Ktok", o3 + 29696, [128, NCH, 256], BF16)
    Vtok = car("Vtok", o3 + 37888, [128, NCH, 256], BF16)
    Stok = car("Stok", o3 + 46080, [128, NCH, 64], BF16)
    Ttok = car("Ttok", o3 + 48128, [128, NCH, 64], F32)
    STbf = car("STbf", o3 + 52224, [64, TOK], BF16)
    tLc = car("tLc", o3 + 56320, [64, NCH], F32)
    AbfS = car("AbfS", o3 + 56448, [128, NCH, 128], BF16)
    hkbS = car("hkbS", o3 + 60544, [128, NCH, 128], BF16)
    hvbS = car("hvbS", o3 + 64640, [64, NCH, 256], BF16)
    gpk = car("gpk", o3 + 72832, [64, 2048], F32)
    ga = gpk[:, 0:512]
    gb = gpk[:, 512:1024]
    gc = gpk[:, 1024:1536]
    gd = gpk[:, 1536:2048]
    hkf = car("hkf", o3 + 81024, [128, 128], F32)
    hvf = car("hvf", o3 + 81536, [64, 256], F32)
    expb = car("expb", o3 + 82560, [128, 384], F32)
    impb = car("impb", o3 + 84096, [128, 384], F32)
    impt = car("impt", o3 + 85632, [128, 384], F32)
    Pc3 = car("Pc3", o3 + 87168, [64, 16], F32)
    PcT = car("PcT", o3 + 87296, [16, 64], F32)
    Pbb = car("Pbb", o3 + 87552, [128, 64], F32)
    initkb = car("initkb", o3 + 87808, [128, 128], BF16)
    inithvb = car("inithvb", o3 + 88064, [64, 256], BF16)
    lgb = car("lgb", o3 + 88576, [128, 64], F32)
    Ebuf3 = car("Ebuf3", o3 + 88832, [128, 64], F32)
    Ehb3 = car("Ehb3", o3 + 89088, [128, 64], BF16)
    ETb3 = car("ETb3", o3 + 89216, [64, 128], BF16)
    STc = car("STc", o3 + 89472, [64, 128], BF16)
    Wbf3 = car("Wbf3", o3 + 89728, [128, 128], BF16)
    stt3 = car("stt3", o3 + 89984, [128, 8], F32)
    ostg = car("ostg", o3 + 90112, [128, 256], F32)
    gnws = car("gnws", o3 + 91136, [1, 256], F32)
    gnwb = car("gnwb", o3 + 92160, [128, 256], F32)
    oT1 = car("oT1", 32768, [128, 4, 2, TOK], BF16)
    wos = car("wos", o3 + 93184, [128, 8, 1024], BF16)
    s1e = [car(f"s1e{i}", o3 + 109568 + i * 4096, [128, 1024], F32)
           for i in range(2)]
    p3q = car("p3q", o3 + 117760, [128, 512], F32)
    hkt3 = car("hkt3", o3 + 119808, [128, 128], F32)
    hvt3 = car("hvt3", o3 + 120320, [64, 256], F32)
    eyerow = car("eyerow", o3 + 121344, [16, 2048], F32)
    impa = car("impa", o3 + 129536, [128, 384], F32)
    p.dma("gpsimd", eyerow[:], eyerowi, gld)

    p.dma("gpsimd", gnws[:], gnwi, gld)
    p.wait("tensor", gld, gld.n)
    p.emit("tensor", lambda t: t.matmul(PB2[:, 512:768], ones1f[:], gnws[:],
                                        start=True, stop=True),
           inc=((mm, 1),))
    p.emit("vector", lambda v: v.tensor_copy(gnwb[:], PB2[:, 512:768]),
           wait=[(mm, mm.n)], inc=((vch, 1),))
    gnw_ready = (vch, vch.n)
    p.dma("gpsimd", wos[:], woi.rearrange("(k p) c -> p k c", p=128), gld)
    wos_ld = gld.n

    def chain(eng, fn, extra=()):
        s_ = vch if eng == "vector" else sch
        p.emit(eng, fn, wait=[(vch, vch.n), (sch, sch.n)] + list(extra),
               inc=((s_, 1),))

    def J():
        if not SERIAL3:
            return
        for eng in ("tensor", "vector", "scalar"):
            for s_ in (mm, vch, sch, tr, ev):
                if s_.n:
                    p.wait(eng, s_, s_.n)

    for h3 in range(4):
        hw_base = gld.n
        p.dma("gpsimd", wq_s3[:],
              wq4[:, h3, :].rearrange("(k p) c -> p k c", p=128), gld,
              wait=[(mm, mm.n)])
        p.dma("gpsimd", wk_s3[:],
              wk4[:, h3, :].rearrange("(k p) c -> p k c", p=128), gld)
        p.dma("gpsimd", wv_s3[:],
              wv4[:, h3, :].rearrange("(k p) c -> p k c", p=128), gld)
        p.dma("gpsimd", wf_s3[:],
              wf4[:, h3, :].rearrange("(k p) c -> p k c", p=128), gld)
        hw_ld = gld.n
        zw = [(vch, vch.n), (sch, sch.n), (mm, mm.n)]
        p.emit("gpsimd", lambda g: g.memset(hkf[:], 0.0), wait=zw,
               inc=((gmc, 1),))
        p.emit("gpsimd", lambda g: g.memset(hvf[:], 0.0), inc=((gmc, 1),))
        p.emit("gpsimd", lambda g: g.memset(expb[:], 0.0), inc=((gmc, 1),))
        zs_v = gmc.n
        p.wait("vector", gmc, zs_v)

        for sl in range(4):
            slb = sl % 2
            tsl = slice(sl * 512, (sl + 1) * 512)
            for kc in range(2):
                for which, wsrc in (("q", wq_s3), ("k", wk_s3), ("v", wv_s3)):
                    waits = [(gld, hw_ld), (ev, ev.n)]
                    for k in range(8):
                        p.emit("tensor", lambda t, k=k, kc=kc, wsrc=wsrc,
                               tsl=tsl:
                               t.matmul(PD[:], wsrc[:, k,
                                                    kc * 128:(kc + 1) * 128],
                                        xn1T[:, k, tsl],
                                        start=(k == 0), stop=(k == 7)),
                               wait=waits if k == 0 else [],
                               inc=((mm, 1),) if k == 7 else ())
                    mv = mm.n
                    if which == "q":
                        p.emit("scalar", lambda s: s.activation(
                            p3q[:], PD[:], AF.Silu),
                            wait=[(mm, mv), (ev, ev.n)], inc=((sch, 1),))
                        p.emit("vector", lambda v, kc=kc, tsl=tsl:
                               v.tensor_scalar(QT3[:, kc, tsl],
                                               p3q[:],
                                               1.0 / 16.0, None, ALU.mult),
                               wait=[(sch, sch.n)], inc=((ev, 1),))
                    elif which == "k":
                        p.emit("scalar", lambda s, kc=kc, slb=slb:
                               s.activation(KTsl[slb][:, kc, :], PD[:],
                                            AF.Silu),
                               wait=[(mm, mv)], inc=((ev, 1),))
                    else:
                        p.emit("vector", lambda v, kc=kc, slb=slb:
                               v.tensor_copy(VTs3[slb][:, kc, :], PD[:]),
                               wait=[(mm, mv)], inc=((ev, 1),))
            kv_done = ev.n
            if STOP_PHASE <= 3.01 and h3 == 0 and sl == 0:
                return finish_dummy()
            if STOP_PHASE <= 3.038 and h3 == 0 and sl == 1:
                return finish_dummy()
            waits = [(vch, vch.n), (sch, sch.n), (ev, ev.n)]
            for k in range(8):
                p.emit("tensor", lambda t, k=k, tsl=tsl:
                       t.matmul(PE2[0:64, :], wf_s3[:, k, :],
                                xn1T[:, k, tsl],
                                start=(k == 0), stop=(k == 7)),
                       wait=waits if k == 0 else [],
                       inc=((mm, 1),) if k == 7 else ())
            p.emit("vector", lambda v: v.tensor_copy(ga, PE2[0:64, :]),
                   wait=[(mm, mm.n)], inc=((vch, 1),))
            chain("scalar", lambda s: s.activation(gb, ga, AF.Abs))
            chain("scalar", lambda s: s.activation(gb, gb, AF.Exp,
                                                   scale=-1.0))
            chain("scalar", lambda s: s.activation(gb, gb, AF.Ln, bias=1.0))
            chain("vector", lambda v: v.tensor_scalar(gd, ga, 0.0, None,
                                                      ALU.min))
            chain("vector", lambda v: v.tensor_tensor(gd, gd, gb,
                                                      ALU.subtract))
            chain("vector", lambda v: v.tensor_scalar(ga, gd, 0.125,
                                                      None, ALU.mult))
            chain("scalar", lambda s: s.activation(gd, ga, AF.Exp))
            chain("vector", lambda v: v.tensor_scalar(gd, gd, -1.0, 1.0,
                                                      ALU.mult, ALU.add))
            for cb in range(4):
                chain("vector", lambda v, cb=cb: v.tensor_tensor_scan(
                    gb[:, cb * 128:(cb + 1) * 128],
                    ga[:, cb * 128:(cb + 1) * 128], zeros64[:], 0.0,
                    ALU.add, ALU.add))
            chain("scalar", lambda s: s.activation(gc, gb, AF.Exp))
            chain("scalar", lambda s: s.activation(ga, gb, AF.Exp,
                                                   scale=-1.0))
            chain("vector", lambda v: v.tensor_mul(gb, gd, ga))
            chain("vector", lambda v, tsl=tsl: v.tensor_copy(
                STbf[:, tsl], gb))
            for cb in range(4):
                ch_ = sl * 4 + cb
                chain("vector", lambda v, cb=cb, ch_=ch_: v.tensor_copy(
                    tLc[:, ch_:ch_ + 1],
                    gc[:, cb * 128 + 127:cb * 128 + 128]))
            gate_v = vch.n
            gate_s = sch.n
            if STOP_PHASE <= 3.02 and h3 == 0 and sl == 0:
                return finish_dummy()
            if STOP_PHASE <= 3.039 and h3 == 0 and sl == 1:
                return finish_dummy()
            for cb in range(4):
                ch_ = sl * 4 + cb
                half = cb % 2
                J()
                skipme = (SKIP3 and h3 == 0 and sl == 1 and cb == 0)
                w = [(vch, gate_v), (sch, gate_s)]
                if sl > 0 or cb >= 2:
                    w.append((tr, tr.n - 7))
                if skipme and 'trans' in SKIP3:
                    continue
                p.emit("tensor", lambda t, ch_=ch_, cb=cb, half=half:
                       t.transpose(PCt[:, half * 1024:half * 1024 + 64],
                                   STbf[:, ch_ * 128:(ch_ + 1) * 128],
                                   ident_s[0:64, 0:64]),
                       wait=w, inc=((mm, 1),))
                p.emit("scalar", lambda s, ch_=ch_, half=half:
                       s.copy(Stok[:, ch_, :],
                              PCt[:, half * 1024:half * 1024 + 64]),
                       wait=[(mm, mm.n)], inc=((tr, 1),))
                p.emit("tensor", lambda t, cb=cb:
                       t.transpose(PA[:, 512:576],
                                   gc[:, cb * 128:(cb + 1) * 128],
                                   identf_s[0:64, 0:64]),
                       wait=[(tr, tr.n)], inc=((mm, 1),))
                p.emit("vector", lambda v, ch_=ch_: v.tensor_copy(
                    Ttok[:, ch_, :], PA[:, 512:576]),
                    wait=[(mm, mm.n)], inc=((tr, 1),))
                p.wait("tensor", tr, tr.n)
                J()
                for kc in range(2):
                    half2 = kc
                    p.emit("tensor", lambda t, cb=cb, kc=kc, slb=slb,
                           half2=half2:
                           t.transpose(PCt[:, half2 * 1024:half2 * 1024 + 128],
                                       KTsl[slb][:, kc,
                                                 cb * 128:(cb + 1) * 128],
                                       ident_s[:]),
                           wait=[(ev, kv_done), (tr, tr.n)], inc=((mm, 1),))
                    p.emit("scalar", lambda s, ch_=ch_, kc=kc, half2=half2:
                           s.copy(Ktok[:, ch_, kc * 128:(kc + 1) * 128],
                                  PCt[:, half2 * 1024:half2 * 1024 + 128]),
                           wait=[(mm, mm.n)], inc=((tr, 1),))
                    p.emit("tensor", lambda t, cb=cb, kc=kc, slb=slb,
                           half2=half2:
                           t.transpose(PCt[:, half2 * 1024 + 128:
                                           half2 * 1024 + 256],
                                       VTs3[slb][:, kc,
                                                 cb * 128:(cb + 1) * 128],
                                       ident_s[:]),
                           wait=[(ev, kv_done)], inc=((mm, 1),))
                    p.emit("scalar", lambda s, ch_=ch_, kc=kc, half2=half2:
                           s.copy(Vtok[:, ch_, kc * 128:(kc + 1) * 128],
                                  PCt[:, half2 * 1024 + 128:
                                      half2 * 1024 + 256]),
                           wait=[(mm, mm.n)], inc=((tr, 1),))
                if skipme and 'abf' in SKIP3:
                    continue
                for kc in range(2):
                    p.emit("tensor", lambda t, kc=kc, slb=slb, cb=cb, ch_=ch_:
                           t.matmul(PA[:, 0:128],
                                    KTsl[slb][:, kc, cb * 128:(cb + 1) * 128],
                                    QT3[:, kc, ch_ * 128:(ch_ + 1) * 128],
                                    start=(kc == 0), stop=(kc == 1)),
                           wait=[(ev, kv_done), (vch, vch.n)] if kc == 0
                           else [],
                           inc=((mm, 1),) if kc == 1 else ())
                p.emit("vector", lambda v, ch_=ch_: v.tensor_tensor(
                    AbfS[:, ch_, :], PA[:, 0:128], triu_s[:], ALU.mult),
                    wait=[(mm, mm.n)], inc=((vch, 1),))
                p.wait("tensor", vch, vch.n)
                J()
                if skipme and 'state' in SKIP3:
                    continue
                for kc in range(2):
                    p.emit("tensor", lambda t, kc=kc, ch_=ch_: t.matmul(
                        PA[:, kc * 64:(kc + 1) * 64],
                        Ktok[:, ch_, kc * 128:(kc + 1) * 128],
                        Stok[:, ch_, :], start=True, stop=True,
                        skip_group_check=True),
                        wait=[(tr, tr.n)] if kc == 0 else [],
                        inc=((mm, 1),) if kc == 1 else ())
                hk_mm = mm.n
                p.emit("tensor", lambda t, ch_=ch_: t.matmul(
                    PB2[0:64, 0:256], Stok[:, ch_, :], Vtok[:, ch_, :],
                    start=True, stop=True), inc=((mm, 1),))
                hv_mm = mm.n
                p.emit("tensor", lambda t, ch_=ch_: t.matmul(
                    PA[:, 512:576], r127_s[:], Ttok[:, ch_, :],
                    start=True, stop=True), inc=((mm, 1),))
                tl_mm = mm.n
                p.emit("vector", lambda v: v.tensor_tensor(
                    hkt3[:], hkf[:], PA[:, 0:128], ALU.add),
                    wait=[(mm, hk_mm)], inc=((vch, 1),))
                p.emit("vector", lambda v: v.tensor_tensor(
                    hkf[:, 0:64], hkt3[:, 0:64], PA[:, 512:576], ALU.mult),
                    wait=[(vch, vch.n), (mm, tl_mm)], inc=((vch, 1),))
                p.emit("vector", lambda v: v.tensor_tensor(
                    hkf[:, 64:128], hkt3[:, 64:128], PA[:, 512:576],
                    ALU.mult),
                    wait=[(vch, vch.n)], inc=((vch, 1),))
                p.emit("vector", lambda v, ch_=ch_: v.tensor_copy(
                    hkbS[:, ch_, :], hkf[:]),
                    wait=[(vch, vch.n)], inc=((vch, 1),))
                p.emit("vector", lambda v: v.tensor_tensor(
                    hvt3[:], hvf[:], PB2[0:64, 0:256], ALU.add),
                    wait=[(mm, hv_mm), (vch, vch.n)], inc=((vch, 1),))
                p.emit("vector", lambda v, ch_=ch_: v.tensor_scalar(
                    hvf[:], hvt3[:],
                    tLc[:, ch_:ch_ + 1], None, ALU.mult),
                    wait=[(vch, vch.n)], inc=((vch, 1),))
                p.emit("vector", lambda v, ch_=ch_: v.tensor_copy(
                    hvbS[:, ch_, :], hvf[:]),
                    wait=[(vch, vch.n)], inc=((vch, 1),))
                p.wait("tensor", vch, vch.n)
                J()
                if STOP_PHASE <= 3.03 and h3 == 0 and sl == 0 and cb == 0:
                    return finish_dummy()
                if STOP_PHASE <= 3.035 and h3 == 0 and sl == 0 and cb == 3:
                    return finish_dummy()
                if STOP_PHASE <= 3.041 and h3 == 0 and sl == 1 and cb == 0:
                    return finish_dummy()
                if STOP_PHASE <= 3.043 and h3 == 0 and sl == 1 and cb == 1:
                    return finish_dummy()
                if STOP_PHASE <= 3.045 and h3 == 0 and sl == 1 and cb == 2:
                    return finish_dummy()
                if STOP_PHASE <= 3.05 and h3 == 0 and sl == 1 and cb == 3:
                    return finish_dummy()
        if STOP_PHASE <= 3.1 and h3 == 0:
            return finish_dummy()
        st_v = vch.n
        p.emit("vector", lambda v: v.tensor_copy(expb[:, 0:128], hkf[:]),
               wait=[(vch, st_v), (gmc, zs_v)], inc=((vch, 1),))
        p.emit("vector", lambda v: v.tensor_copy(
            expb[0:64, 128:384], hvf[:]),
            wait=[(vch, vch.n)], inc=((vch, 1),))
        p.dma("gpsimd", ag3s[h3], expb[:], vst, wait=[(vch, vch.n)])
        p.emit("gpsimd", lambda g, h3=h3: g.collective_compute(
            "AllGather", ALU.bypass, replica_groups=[list(range(N_CORES))],
            ins=[ag3s[h3]], outs=[ag3o[h3]]),
            wait=[(vst, vst.n)], inc=((cc, 1),))
        cc_st = cc.n
        for ri in range(8):
            p.dma("sync", impt[:], ag3o[h3][ri * 128:(ri + 1) * 128, :],
                  sld, wait=[(cc, cc_st)] if ri == 0 else [(vch, vch.n)])
            if ri == 0:
                p.emit("vector", lambda v: v.tensor_scalar(
                    impa[:], impt[:], flags[:, 8:9], None, ALU.mult),
                    wait=[(sld, sld.n)], inc=((vch, 1),))
            else:
                dst = impb if ri % 2 else impa
                src = impa if ri % 2 else impb
                p.emit("vector", lambda v, ri=ri, dst=dst, src=src:
                       v.scalar_tensor_tensor(
                           dst[:], impt[:], flags[:, 8 + ri:9 + ri], src[:],
                           ALU.mult, ALU.add),
                       wait=[(sld, sld.n)], inc=((vch, 1),))
        imp_v = vch.n
        if STOP_PHASE <= 3.2 and h3 == 0:
            return finish_dummy()
        p.emit("gpsimd", lambda g: g.memset(Pc3[:, 0:1], 1.0),
               wait=[(vch, vch.n)], inc=((gmc, 1),))
        p.wait("vector", gmc, gmc.n)
        for ch_ in range(1, 16):
            p.emit("vector", lambda v, ch_=ch_: v.tensor_mul(
                Pc3[:, ch_:ch_ + 1], Pc3[:, ch_ - 1:ch_],
                tLc[:, ch_ - 1:ch_]),
                wait=[(vch, vch.n)], inc=((vch, 1),))
        p.emit("tensor", lambda t: t.transpose(
            PA[0:16, 512:576], Pc3[:], identf_s[0:64, 0:64]),
            wait=[(vch, vch.n), (mm, mm.n)], inc=((mm, 1),))
        p.emit("vector", lambda v: v.tensor_copy(PcT[:], PA[0:16, 512:576]),
               wait=[(mm, mm.n)], inc=((vch, 1),))
        pct_v = vch.n
        p.wait("tensor", vch, pct_v)
        if STOP_PHASE <= 3.3 and h3 == 0:
            return finish_dummy()

        for ch_ in range(NCH):
            csl = slice(ch_ * 128, (ch_ + 1) * 128)
            p.emit("tensor", lambda t, ch_=ch_: t.matmul(
                PA[:, 512:576], eyerow[:, ch_ * 128:(ch_ + 1) * 128],
                PcT[:], start=True, stop=True),
                wait=[(vch, vch.n), (sch, sch.n)], inc=((mm, 1),))
            p.emit("vector", lambda v: v.tensor_copy(Pbb[:], PA[:, 512:576]),
                   wait=[(mm, mm.n)], inc=((vch, 1),))
            for kc in range(2):
                p.emit("vector", lambda v, kc=kc: v.tensor_tensor(
                    initkb[:, kc * 64:(kc + 1) * 64],
                    impb[:, kc * 64:(kc + 1) * 64], Pbb[:, 0:64], ALU.mult),
                    wait=[(vch, vch.n), (mm, mm.n)], inc=((vch, 1),))
            p.emit("vector", lambda v, ch_=ch_: v.tensor_scalar(
                inithvb[:], impb[0:64, 128:384], Pc3[:, ch_:ch_ + 1],
                None, ALU.mult),
                wait=[(vch, vch.n), (vch, imp_v), (mm, mm.n)],
                inc=((vch, 1),))
            corr_v = vch.n
            hkp = (z128b if ch_ == 0 else hkbS[:, ch_ - 1, :])
            p.emit("tensor", lambda t, ch_=ch_: t.matmul(
                PB2[:, 0:64], AbfS[:, ch_, :], Stok[:, ch_, :],
                start=True, stop=False),
                wait=[(vch, corr_v)])
            for kc in range(2):
                p.emit("tensor", lambda t, kc=kc, hkp=hkp, csl=csl: t.matmul(
                    PB2[:, 0:64], QT3[:, kc, csl],
                    hkp[:, kc * 64:(kc + 1) * 64], start=False, stop=False))
            for kc in range(2):
                p.emit("tensor", lambda t, kc=kc, csl=csl: t.matmul(
                    PB2[:, 0:64], QT3[:, kc, csl],
                    initkb[:, kc * 64:(kc + 1) * 64],
                    start=False, stop=(kc == 1)),
                    inc=((mm, 1),) if kc == 1 else ())
            p.emit("vector", lambda v, ch_=ch_: v.tensor_tensor(
                lgb[:], PB2[:, 0:64], Ttok[:, ch_, :], ALU.mult),
                wait=[(mm, mm.n)], inc=((vch, 1),))
            p.emit("vector", lambda v: v.reduce_max(stt3[:, 0:1], lgb[:],
                                                    axis=AX.X),
                   wait=[(vch, vch.n)], inc=((vch, 1),))
            p.emit("vector", lambda v: v.tensor_scalar(
                lgb[:], lgb[:], stt3[:, 0:1], None, ALU.subtract),
                wait=[(vch, vch.n)], inc=((vch, 1),))
            p.emit("scalar", lambda s: s.activation(
                Ebuf3[:], lgb[:], AF.Exp, accum_out=stt3[:, 1:2]),
                wait=[(vch, vch.n)], inc=((sch, 1),))
            p.emit("vector", lambda v: v.reciprocal(stt3[:, 2:3],
                                                    stt3[:, 1:2]),
                   wait=[(sch, sch.n)], inc=((vch, 1),))
            p.emit("vector", lambda v, ch_=ch_: v.tensor_tensor(
                Ehb3[:], Ebuf3[:], Ttok[:, ch_, :], ALU.mult),
                wait=[(vch, vch.n - 1), (sch, sch.n)], inc=((vch, 1),))
            ehb_v = vch.n
            p.emit("tensor", lambda t: t.transpose(PCt[0:64, 0:128], Ehb3[:],
                                                   ident_s[:]),
                   wait=[(vch, ehb_v), (tr, tr.n)], inc=((mm, 1),))
            p.emit("scalar", lambda s: s.copy(ETb3[:], PCt[0:64, 0:128]),
                   wait=[(mm, mm.n)], inc=((tr, 1),))
            p.emit("tensor", lambda t, csl=csl: t.matmul(
                PA[:, 0:128], STbf[:, csl], ETb3[:], start=True, stop=True),
                wait=[(tr, tr.n)], inc=((mm, 1),))
            p.emit("vector", lambda v: v.tensor_tensor(
                Wbf3[:], PA[:, 0:128], triu_s[:], ALU.mult),
                wait=[(mm, mm.n)], inc=((vch, 1),))
            wbf_v = vch.n
            hvp = (z64b if ch_ == 0 else hvbS[:, ch_ - 1, :])
            p.emit("tensor", lambda t, ch_=ch_: t.matmul(
                PB2[:, 512:768], Wbf3[:], Vtok[:, ch_, :],
                start=True, stop=False),
                wait=[(vch, wbf_v)])
            p.emit("tensor", lambda t, hvp=hvp: t.matmul(
                PB2[:, 512:768], ETb3[:], hvp[:], start=False, stop=False))
            p.emit("tensor", lambda t: t.matmul(
                PB2[:, 512:768], ETb3[:], inithvb[:], start=False, stop=True),
                inc=((mm, 1),))
            o_mm = mm.n
            p.emit("scalar", lambda s: s.activation(
                p3q[:, 0:256], PB2[:, 512:768], AF.Square,
                accum_out=stt3[:, 3:4]),
                wait=[(mm, o_mm)], inc=((sch, 1),))
            p.emit("vector", lambda v: v.tensor_tensor(
                stt3[:, 4:5], stt3[:, 2:3], stt3[:, 2:3], ALU.mult),
                wait=[(sch, sch.n)], inc=((vch, 1),))
            p.emit("vector", lambda v: v.tensor_tensor(
                stt3[:, 4:5], stt3[:, 3:4], stt3[:, 4:5], ALU.mult),
                wait=[(vch, vch.n)], inc=((vch, 1),))
            p.emit("vector", lambda v: v.tensor_scalar(
                stt3[:, 4:5], stt3[:, 4:5], 1.0 / 256.0, None, ALU.mult),
                wait=[(vch, vch.n)], inc=((vch, 1),))
            p.emit("scalar", lambda s: s.activation(
                stt3[:, 5:6], stt3[:, 4:5], AF.Sqrt, bias=1e-5),
                wait=[(vch, vch.n)], inc=((sch, 1),))
            p.emit("vector", lambda v: v.reciprocal(stt3[:, 6:7],
                                                    stt3[:, 5:6]),
                   wait=[(sch, sch.n)], inc=((vch, 1),))
            p.emit("vector", lambda v: v.tensor_tensor(
                stt3[:, 7:8], stt3[:, 6:7], stt3[:, 2:3], ALU.mult),
                wait=[(vch, vch.n)], inc=((vch, 1),))
            p.emit("vector", lambda v: v.tensor_scalar(
                ostg[:], PB2[:, 512:768], stt3[:, 7:8], None, ALU.mult),
                wait=[(vch, vch.n)], inc=((vch, 1),))
            p.emit("vector", lambda v: v.tensor_tensor(
                ostg[:], ostg[:], gnwb[:], ALU.mult),
                wait=[(vch, vch.n), gnw_ready], inc=((vch, 1),))
            og_v = vch.n
            for vc in range(2):
                p.emit("tensor", lambda t, vc=vc: t.transpose(
                    PE2[:, 0:128] if vc == 0 else PD[:, 0:128],
                    ostg[:, vc * 128:(vc + 1) * 128], identf_s[:]),
                    wait=[(vch, og_v), (tr, tr.n)] if vc == 0 else [],
                    inc=((mm, 1),))
                p.emit("vector", lambda v, vc=vc, h3=h3, csl=csl:
                       v.tensor_copy(oT1[:, h3, vc, csl],
                                     PE2[:, 0:128] if vc == 0
                                     else PD[:, 0:128]),
                       wait=[(mm, mm.n)], inc=((tr, 1),))
            p.wait("tensor", tr, tr.n)
            if STOP_PHASE <= 3.4 and h3 == 0 and ch_ == 0:
                return finish_dummy()
        if STOP_PHASE <= 3.5 and h3 == 0:
            return finish_dummy()
    p.wait("tensor", gld, wos_ld)
    for i in range(16):
        pb = i % 2
        csl = slice(i * 128, (i + 1) * 128)
        for hv in range(8):
            h3, vc = hv // 2, hv % 2
            w = []
            if hv == 0:
                w = [(tr, tr.n)]
                if i >= 2:
                    w.append((vst, vst.n - 16))
            for half in range(2):
                ps_ = PD if half == 0 else PE2
                p.emit("tensor", lambda t, h3=h3, vc=vc, half=half, ps_=ps_,
                       csl=csl, hv=hv:
                       t.matmul(ps_[:], oT1[:, h3, vc, csl],
                                wos[:, hv, half * 512:(half + 1) * 512],
                                start=(hv == 0), stop=(hv == 7)),
                       wait=w if half == 0 else [],
                       inc=((mm, 1),) if (hv == 7 and half == 1) else ())
        mv = mm.n
        p.emit("vector", lambda v, pb=pb: v.tensor_copy(
            s1e[pb][:, 0:512], PD[:]), wait=[(mm, mv)], inc=((vch, 1),))
        p.emit("vector", lambda v, pb=pb: v.tensor_copy(
            s1e[pb][:, 512:1024], PE2[:]), wait=[(mm, mv)], inc=((vch, 1),))
        p.dma("gpsimd", s1sp[i * 128:(i + 1) * 128, :], s1e[pb][:], vst,
              wait=[(vch, vch.n)])
        p.wait("tensor", vch, vch.n)
    if STOP_PHASE <= 4:
        return finish_dummy()

    barrier()

    I4 = 2816
    NI = 22
    w1s4 = car("w1s4", 0, [128, 8, I4], BF16)
    w3s4 = car("w3s4", 45056, [128, 8, I4], BF16)
    w2s4 = car("w2s4", 90112, [128, NI, 1024], BF16)
    prodT = car("prodT", 135168, [128, NI, 512], BF16)
    xs4 = [car(f"xs4_{i}", 157696 + i * 8192, [128, 8, 512], BF16)
           for i in range(2)]
    h1s4 = car("h1s4", 174080, [128, 512], F32)
    zs4 = [car(f"zs4_{i}", 176128 + i * 2048, [128, 1024], BF16)
           for i in range(2)]
    wts4 = car("wts4", 180224, [128, 128], F32)
    wtt = [car(f"wtt{i}", 180736 + i * 64, [128, 16], F32) for i in range(2)]

    w1_base4 = gld.n
    for cg in range(2):
        p.dma("gpsimd", w1s4[:, :, cg * 1408:(cg + 1) * 1408],
              ew1t[:, cg * 1408:(cg + 1) * 1408]
              .rearrange("(k p) f -> p k f", p=128), gld)
    w3_base4 = gld.n
    for cg in range(2):
        p.dma("gpsimd", w3s4[:, :, cg * 1408:(cg + 1) * 1408],
              ew3t[:, cg * 1408:(cg + 1) * 1408]
              .rearrange("(k p) f -> p k f", p=128), gld)
    for cg in range(2):
        p.dma("gpsimd", w2s4[:, cg * 11:(cg + 1) * 11, :],
              ew2t[cg * 1408:(cg + 1) * 1408, :]
              .rearrange("(f p) c -> p f c", p=128), gld)
    w2_done4 = gld.n

    wacc = [car(f"wacc{i}", 180992 + i * 64, [128, 16], F32)
            for i in range(2)]
    for ri in range(8):
        for e in range(8):
            j = ri * 8 + e
            p.dma("sync", wtt[j % 2][:],
                  ag2o[ri * 1024 + e * 128:ri * 1024 + e * 128 + 128, :],
                  sld, wait=[(cc, CC_AG2)] if j == 0 else [(vch, vch.n - 1)])
            if e == 0:
                p.emit("vector", lambda v, j=j: v.tensor_scalar(
                    wacc[0][:], wtt[j % 2][:],
                    flagwe[:, j:j + 1], None, ALU.mult),
                    wait=[(sld, sld.n)], inc=((vch, 1),))
            else:
                dst = wacc[e % 2]
                src = wacc[(e - 1) % 2]
                p.emit("vector", lambda v, j=j, dst=dst, src=src:
                       v.scalar_tensor_tensor(
                           dst[:], wtt[j % 2][:], flagwe[:, j:j + 1],
                           src[:], ALU.mult, ALU.add),
                       wait=[(sld, sld.n)], inc=((vch, 1),))
        p.emit("vector", lambda v, ri=ri: v.tensor_copy(
            wts4[:, ri * 16:(ri + 1) * 16], wacc[1][:]),
            wait=[(vch, vch.n)], inc=((vch, 1),))
    wt_v = vch.n

    zv_vals = []
    ag4v = ag4o.rearrange("(r k p) t -> r p k t", r=8, p=128)
    for s4 in range(32):
        r4, ls = s4 // 4, s4 % 4
        xs = xs4[s4 % 2]
        w = [(cc, CC_AG4)]
        if s4 >= 2:
            w.append((mm, mm.n))
        p.dma("sync", xs[:], ag4v[r4][:, :, ls * 512:(ls + 1) * 512],
              ld, wait=w)
        ldv = ld.n
        for f in range(NI):
            cgf = min(2, f // 11 + 1)
            for which in range(2):
                pb_ = PD if which == 0 else PE2
                wsrc = w1s4 if which == 0 else w3s4
                wg = (w1_base4 if which == 0 else w3_base4) + 16 * cgf
                for k in range(8):
                    waits = []
                    if k == 0:
                        waits = [(ld, ldv), (gld, wg)]
                        if s4 > 0 or f > 0:
                            waits.append((sch, sch.n) if which == 0
                                         else (ev, ev.n))
                    p.emit("tensor", lambda t, f=f, k=k, xs=xs, pb_=pb_,
                           wsrc=wsrc:
                           t.matmul(pb_[:], wsrc[:, k, f * 128:(f + 1) * 128],
                                    xs[:, k, :], start=(k == 0),
                                    stop=(k == 7)),
                           wait=waits, inc=((mm, 1),) if k == 7 else ())
                if which == 0:
                    p.emit("scalar", lambda sc: sc.activation(
                        h1s4[:], PD[:], AF.Silu),
                        wait=[(mm, mm.n), (ev, ev.n)], inc=((sch, 1),))
            p.emit("vector", lambda v, f=f: v.tensor_tensor(
                prodT[:, f, :], h1s4[:], PE2[:], ALU.mult),
                wait=[(mm, mm.n), (sch, sch.n)], inc=((ev, 1),))
        prod_done = ev.n
        for tt in range(4):
            gt = s4 * 4 + tt
            pz = PA if tt % 2 == 0 else PB2
            for f in range(NI):
                waits = []
                if f == 0:
                    waits = [(ev, prod_done), (gld, w2_done4)]
                    if gt >= 2:
                        waits.append((ev, zv_vals[gt - 2]))
                for half in range(2):
                    p.emit("tensor", lambda t, f=f, tt=tt, half=half, pz=pz:
                           t.matmul(pz[:, half * 512:(half + 1) * 512],
                                    prodT[:, f, tt * 128:(tt + 1) * 128],
                                    w2s4[:, f, half * 512:(half + 1) * 512],
                                    start=(f == 0), stop=(f == NI - 1)),
                           wait=waits if (f == 0 and half == 0) else [],
                           inc=((mm, 1),)
                           if (f == NI - 1 and half == 1) else ())
            mv = mm.n
            zb = zs4[gt % 2]
            wv_ = [(mm, mv)]
            if gt >= 2:
                wv_.append((vst, vst.n - 16))
            p.emit("vector", lambda v, zb=zb, pz=pz, gt=gt:
                   v.tensor_scalar(zb[:], pz[:], wts4[:, gt:gt + 1],
                                   None, ALU.mult),
                   wait=wv_, inc=((ev, 1),))
            zv_vals.append(ev.n)
            p.dma("gpsimd", zstage[gt * 128:(gt + 1) * 128, :], zb[:], vst,
                  wait=[(ev, ev.n)])
    p.emit("gpsimd", lambda g: g.collective_compute(
        "ReduceScatter", ALU.add, replica_groups=[list(range(N_CORES))],
        ins=[zstage], outs=[rsz]), wait=[(vst, vst.n)], inc=((cc, 1),))
    CC_RS = cc.n
    if STOP_PHASE <= 5:
        return finish_dummy()

    barrier()

    x1t = [car(f"x1t{i}", 0 + i * 4096, [128, D], F32) for i in range(2)]
    s1t = [car(f"s1t{i}", 8192 + i * 4096, [128, D], F32) for i in range(2)]
    rst = [car(f"rst{i}", 16384 + i * 2048, [128, D], BF16) for i in range(2)]
    ffb = car("ffb", 20480, [128, D], F32)
    gb5 = car("gb5", 24576, [128, D], F32)
    ot5 = [car(f"ot5{i}", 28672 + i * 4096, [128, D], F32) for i in range(2)]
    twi5 = car("twi5", 36864, [128, 16], F32)

    p.emit("vector", lambda v: v.reciprocal(twi5[:], tw_s[:]),
           inc=((vch, 1),))
    if DBG_OUT:
        p.dma("sync", rszdbg, rsz, sto, wait=[(cc, CC_RS)])
    for i in range(16):
        pb = i % 2
        sl_ = slice(i * 128, (i + 1) * 128)
        w = [(sto, 16 * (i - 1))] if i >= 2 else [(cc, CC_RS)]
        p.dma("sync", x1t[pb][:], x1sp[sl_, :], ld, wait=w)
        p.dma("sync", s1t[pb][:], s1sp[sl_, :], ld)
        p.dma("sync", rst[pb][:], rsz[sl_, :], ld)
        ldv = ld.n
        p.emit("vector", lambda v, pb=pb, i=i: v.tensor_scalar(
            ffb[:], rst[pb][:], twi5[:, i:i + 1], None, ALU.mult),
            wait=[(ld, ldv), (vch, vch.n)], inc=((vch, 1),))
        p.emit("scalar", lambda s: s.activation(gb5[:], ffb[:], AF.Silu),
               wait=[(vch, vch.n)], inc=((sch, 1),))
        p.emit("vector", lambda v, pb=pb: v.tensor_tensor(
            ot5[pb][:], s1t[pb][:], gb5[:], ALU.mult),
            wait=[(sch, sch.n)], inc=((vch, 1),))
        p.emit("vector", lambda v, pb=pb: v.tensor_tensor(
            ot5[pb][:], ot5[pb][:], x1t[pb][:], ALU.add),
            wait=[(vch, vch.n)], inc=((vch, 1),))
        p.dma("sync", out_ext[sl_, :], ot5[pb][:], sto, wait=[(vch, vch.n)])
    p.wait("sync", sto, sto.n)
    p.wait("vector", sto, sto.n)

    p.finalize()
    return nc



def _consts():
    import ml_dtypes
    ident = np.eye(128).astype(ml_dtypes.bfloat16)
    identf = np.eye(128).astype(np.float32)
    tk = np.arange(128)[:, None]
    tq = np.arange(128)[None, :]
    triu = (tk <= tq).astype(ml_dtypes.bfloat16)
    r127 = np.zeros((128, 128), np.float32)
    r127[127, :] = 1.0
    return ident, identf, triu, r127


def kernel(L0_attn_x, L0_attn_W_attn, L0_attn_W_proj, L0_ffn_router_w,
           L0_ffn_router_b, L0_ffn_w1, L0_ffn_b1, L0_ffn_w2, L0_ffn_b2,
           L0_ffn_gamma, L0_ffn_beta, L1_attn_Wq, L1_attn_Wk, L1_attn_Wv,
           L1_attn_Wf, L1_attn_g_norm_w, L1_attn_Wo, L1_ffn_router_w,
           L1_ffn_routing_bias, L1_ffn_sw1, L1_ffn_sw2, L1_ffn_sw3,
           L1_ffn_ew1, L1_ffn_ew2, L1_ffn_ew3):
    del L0_ffn_router_w, L0_ffn_router_b
    asc = np.ascontiguousarray
    x = np.asarray(L0_attn_x, np.float32)
    W_attn = np.asarray(L0_attn_W_attn, np.float32)
    wproj = np.asarray(L0_attn_W_proj, np.float32)
    ident, identf, triu, r127 = _consts()
    w1 = asc(np.asarray(L0_ffn_w1, np.float32)[0])
    b1 = asc(np.asarray(L0_ffn_b1, np.float32)[0].reshape(32, 128).T)
    w2 = asc(np.asarray(L0_ffn_w2, np.float32)[0])
    b2 = asc(np.asarray(L0_ffn_b2, np.float32)[0].reshape(1, D))
    gamma = np.asarray(L0_ffn_gamma, np.float32).reshape(1, D)
    beta = np.asarray(L0_ffn_beta, np.float32).reshape(1, D)
    rw_pad = np.zeros((D, 8), np.float32)
    rw_pad[:, :7] = np.asarray(L1_ffn_router_w, np.float32).T
    rb_pad = np.full((1, 8), -1e9, np.float32)
    rb_pad[0, :7] = np.asarray(L1_ffn_routing_bias, np.float32)
    Wq = np.asarray(L1_attn_Wq, np.float32).reshape(D, 4, 256)
    Wk = np.asarray(L1_attn_Wk, np.float32).reshape(D, 4, 256)
    Wv = np.asarray(L1_attn_Wv, np.float32).reshape(D, 4, 256)
    Wf = np.asarray(L1_attn_Wf, np.float32).reshape(D, 4, 64)
    gnw = np.asarray(L1_attn_g_norm_w, np.float32).reshape(1, 256)
    wo = np.asarray(L1_attn_Wo, np.float32)
    ew1 = np.asarray(L1_ffn_ew1, np.float32)
    ew2 = np.asarray(L1_ffn_ew2, np.float32)
    ew3 = np.asarray(L1_ffn_ew3, np.float32)
    sw1 = np.asarray(L1_ffn_sw1, np.float32)
    sw2 = np.asarray(L1_ffn_sw2, np.float32)
    sw3 = np.asarray(L1_ffn_sw3, np.float32)

    eyerow_np = np.zeros((16, 2048), np.float32)
    for ch in range(16):
        eyerow_np[ch, ch * 128:(ch + 1) * 128] = 1.0

    nc = build_mega()
    in_maps = []
    for c in range(N_CORES):
        b = c // 2
        hh = c % 2
        heads = range(hh * 8, hh * 8 + 8)
        wqk_a = np.stack([np.concatenate(
            [W_attn[:, h * 64:(h + 1) * 64],
             W_attn[:, 1024 + h * 64:1024 + (h + 1) * 64]], axis=1)
            for h in heads])
        wv_a = np.stack([W_attn[:, 2048 + h * 64:2048 + (h + 1) * 64]
                         for h in heads])
        wproj_perm = np.concatenate(
            [wproj[hh * 512:(hh + 1) * 512],
             wproj[(1 - hh) * 512:(2 - hh) * 512]], axis=0)
        flags = np.zeros((128, 32), np.float32)
        flags[:, c ^ 1] = 1.0
        if hh == 1:
            flags[:, 8 + (c - 1)] = 1.0
        flags[:, 16] = 1.0 if hh == 0 else 0.0
        flags[:, 17] = 1.0 if hh == 1 else 0.0
        flagwe = np.zeros((128, 64), np.float32)
        e_mine = c if c < 7 else 7
        for ri in range(8):
            flagwe[:, ri * 8 + e_mine] = 1.0
        if c < 7:
            w1e, w2e, w3e = ew1[c], ew2[c], ew3[c]
        else:
            w1e, w2e, w3e = sw1, sw2, sw3
        in_maps.append({
            "xb": asc(x[b]),
            "xown": asc(x[b, hh * TOK:(hh + 1) * TOK]),
            "wqk": asc(wqk_a), "wvh": asc(wv_a),
            "w1i": w1, "b1i": b1, "w2i": w2, "b2i": b2,
            "gami": gamma, "beti": beta,
            "wproji": asc(wproj_perm),
            "rwi": rw_pad, "rbi": rb_pad,
            "wq4": asc(Wq), "wk4": asc(Wk), "wv4": asc(Wv), "wf4": asc(Wf),
            "gnwi": gnw, "woi": wo,
            "ew1t": asc(w1e.T), "ew3t": asc(w3e.T), "ew2t": asc(w2e.T),
            "identi": ident, "identfi": identf, "triui": triu,
            "r127i": r127, "eyerowi": eyerow_np,
            "flagsi": flags, "flagwei": flagwe,
        })
    res = run_bass_kernel_spmd(nc, in_maps, core_ids=list(range(N_CORES)))
    out = np.zeros((B, T, D), np.float32)
    for c in range(N_CORES):
        b = c // 2
        hh = c % 2
        out[b, hh * TOK:(hh + 1) * TOK] = np.asarray(res.results[c]["out"])
    return out
